# version 05 (final)
# baseline (speedup 1.0000x reference)
import numpy as np
import ml_dtypes
from contextlib import ExitStack

import concourse.bass as bass
import concourse.mybir as mybir
from concourse.bass_utils import run_bass_kernel_spmd

F32 = mybir.dt.float32
BF16 = mybir.dt.bfloat16
U8 = mybir.dt.uint8
AF = mybir.ActivationFunctionType
ALU = mybir.AluOpType
AX = mybir.AxisListType

D = 2048
H = 16
HD = 128
KV = 4
FF = 4096
NE = 8
L = 2048
TH = 1024
EPS = 1e-5
RSQ = float(HD) ** -0.5

ENGS = ("pe", "act", "dve", "pool", "sp")
DMA_ENGS = ("sp", "act", "pool")


class Slot:
    __slots__ = ("w", "r")

    def __init__(self):
        self.w = None
        self.r = []


class Rec:
    __slots__ = ("eng", "fn", "deps", "dma", "need", "sem", "val", "waits")

    def __init__(self, eng, fn, deps, dma):
        self.eng = eng
        self.fn = fn
        self.deps = deps
        self.dma = dma
        self.need = False
        self.sem = None
        self.val = 0
        self.waits = None


class KB:
    def __init__(self):
        self.recs = {e: [] for e in ENGS}
        self.slots = {}
        self.n_dma_sems = 4
        self.dead = False

    def S(self, name):
        s = self.slots.get(name)
        if s is None:
            s = Slot()
            self.slots[name] = s
        return s

    def op(self, eng, fn, reads=(), writes=(), dma=False, deps=()):
        if self.dead:
            return Rec(eng, fn, set(), dma)
        ds = set()
        for s in reads:
            if s.w is not None:
                ds.add(s.w)
        for s in writes:
            ds.update(s.r)
            if s.w is not None:
                ds.add(s.w)
        for d in deps:
            if d is not None:
                ds.add(d)
        rec = Rec(eng, fn, ds, dma)
        self.recs[eng].append(rec)
        for s in reads:
            s.r.append(rec)
        for s in writes:
            s.w = rec
            s.r = []
        return rec

    def barrier(self):
        lasts = {e: (self.recs[e][-1] if self.recs[e] else None) for e in ENGS}
        marks = {}
        for e in ENGS:
            m = self.op(e, lambda h: h.nop(), deps=[lasts[e]] if lasts[e] is not None else [])
            m.need = True
            marks[e] = m
        for e in ENGS:
            self.op(e, lambda h: h.nop(), deps=list(marks.values()))
        self.slots = {}

    def finalize(self, nc, ctx):
        for e in ENGS:
            for rec in self.recs[e]:
                for d in rec.deps:
                    d.need = True
        semh = {}
        for e in ENGS:
            semh["c_" + e] = ctx.enter_context(nc.semaphore("c_" + e))
        for e in DMA_ENGS:
            for i in range(self.n_dma_sems):
                semh[f"d_{e}{i}"] = ctx.enter_context(nc.semaphore(f"d_{e}{i}"))
        self.semh = semh
        counts = {}
        rr = {e: 0 for e in ENGS}
        for e in ENGS:
            for rec in self.recs[e]:
                if not rec.need:
                    continue
                if rec.dma:
                    nm = f"d_{e}{rr[e] % self.n_dma_sems}"
                    rr[e] += 1
                    counts[nm] = counts.get(nm, 0) + 16
                else:
                    nm = "c_" + e
                    counts[nm] = counts.get(nm, 0) + 1
                rec.sem, rec.val = nm, counts[nm]
        for e in ENGS:
            for rec in self.recs[e]:
                w = {}
                for d in rec.deps:
                    if w.get(d.sem, 0) < d.val:
                        w[d.sem] = d.val
                rec.waits = w

    def emit(self, eng, h):
        hw = {}
        for rec in self.recs[eng]:
            for sem, val in sorted(rec.waits.items()):
                if hw.get(sem, 0) < val:
                    h.wait_ge(self.semh[sem], val)
                    hw[sem] = val
            inst = rec.fn(h)
            if rec.need:
                inst.then_inc(self.semh[rec.sem], 16 if rec.dma else 1)


def host_consts():
    c = {}
    c["id_bf"] = np.eye(128).astype(ml_dtypes.bfloat16)
    c["id_f32"] = np.eye(128).astype(np.float32)
    P = np.zeros((128, 128), np.float32)
    for d_ in range(128):
        P[(d_ + 64) % 128, d_] = 1.0
    c["swapP"] = P.astype(ml_dtypes.bfloat16)
    c["maskA"] = (np.arange(128)[:, None] <= np.arange(128)[None, :]).astype(np.uint8)
    v = np.arange(2048)[None, :] - 1024
    c["emaster"] = (v < np.arange(128)[:, None]).astype(np.uint8)
    inv = 1.0 / (1e6 ** (np.arange(0, HD, 2, dtype=np.float64) / HD))
    ang = np.arange(L, dtype=np.float64)[None, :] * inv[:, None]
    cos = np.cos(ang)
    sin = np.sin(ang)
    c["ropec"] = np.concatenate([cos, cos], 0).astype(np.float32)
    c["ropes"] = np.concatenate([-sin, sin], 0).astype(np.float32)
    c["i99"] = (np.tile(np.arange(8, dtype=np.float32), (128, 1)) - 99.0)
    return c


def build(limit=None, p2_pairs=8, p2_chunks=8, p2_mask=7, tap=None):
    nc = bass.Bass("TRN2", target_bir_lowering=False)
    kb = KB()
    S = kb.S
    ctx = ExitStack()

    hX = nc.dram_tensor("x", [L, D], F32, kind="ExternalInput")
    hGW = {nm: nc.dram_tensor("g" + nm, [D, D], F32, kind="ExternalInput")
           for nm in ("Wq", "Wk", "Wv", "Wg", "Wo")}
    hGam = nc.dram_tensor("gamma", [128, 16], F32, kind="ExternalInput")
    hBet = nc.dram_tensor("beta", [128, 16], F32, kind="ExternalInput")
    hN1 = nc.dram_tensor("n1w", [128, 16], F32, kind="ExternalInput")
    hN2 = nc.dram_tensor("n2w", [128, 16], F32, kind="ExternalInput")
    hFq = nc.dram_tensor("fwq", [D, D], F32, kind="ExternalInput")
    hFk = nc.dram_tensor("fwk", [D, 512], F32, kind="ExternalInput")
    hFv = nc.dram_tensor("fwv", [D, 512], F32, kind="ExternalInput")
    hFo = nc.dram_tensor("fwo", [D, D], F32, kind="ExternalInput")
    hGate = nc.dram_tensor("gatew", [D, NE], F32, kind="ExternalInput")
    hW1 = nc.dram_tensor("w1", [NE, D, FF], F32, kind="ExternalInput")
    hW3 = nc.dram_tensor("w3", [NE, D, FF], F32, kind="ExternalInput")
    hW2 = nc.dram_tensor("w2", [NE, FF, D], F32, kind="ExternalInput")
    hIdb = nc.dram_tensor("id_bf", [128, 128], BF16, kind="ExternalInput")
    hIdf = nc.dram_tensor("id_f32", [128, 128], F32, kind="ExternalInput")
    hSwp = nc.dram_tensor("swapP", [128, 128], BF16, kind="ExternalInput")
    hMA = nc.dram_tensor("maskA", [128, 128], U8, kind="ExternalInput")
    hEM = nc.dram_tensor("emaster", [128, 2048], U8, kind="ExternalInput")
    hRc = nc.dram_tensor("ropec", [128, 2048], F32, kind="ExternalInput")
    hRs = nc.dram_tensor("ropes", [128, 2048], F32, kind="ExternalInput")
    hI99 = nc.dram_tensor("i99", [128, 8], F32, kind="ExternalInput")
    hOut = nc.dram_tensor("out", [L, D], F32, kind="ExternalOutput")
    hDbgF = nc.dram_tensor("dbgf", [128, 16384], F32, kind="ExternalOutput")
    hDbgB = nc.dram_tensor("dbgb", [128, 16384], BF16, kind="ExternalOutput")

    sWB = {nm: nc.dram_tensor("s" + nm, [D, D], BF16) for nm in ("Wq", "Wk", "Wv", "Wg", "Wo")}
    sXb = nc.dram_tensor("sXb", [TH, D], BF16)
    sXt1 = nc.dram_tensor("sXt1", [TH, D], BF16)
    sH1 = nc.dram_tensor("sH1", [TH, D], F32)
    sKc = nc.dram_tensor("sKc", [128, KV * TH], BF16)
    sVc = nc.dram_tensor("sVc", [128, 8 * 512], BF16)

    def alloc(name, cols, dt, at):
        return nc.alloc_sbuf_tensor_at(name, [128, cols], dt, offset=at)

    BASE = 16512
    o = [BASE]

    def calloc(name, cols, dt):
        nb = cols * {F32: 4, BF16: 2, U8: 1}[dt]
        at = o[0]
        o[0] = at + ((nb + 31) // 32) * 32
        return alloc(name, cols, dt, at)

    S32 = calloc("S32", H * 128, F32)
    Sbf = calloc("Sbf", H * 128, BF16)
    em = calloc("em", 2048, U8)
    zf = calloc("zf", 1024, F32)
    zb = calloc("zb", 1024, BF16)
    mA = calloc("mA", 128, U8)
    idb = calloc("idb", 128, BF16)
    idf = calloc("idf", 128, F32)
    swp = calloc("swp", 128, BF16)
    gam = calloc("gam", 16, F32)
    bet = calloc("bet", 16, F32)
    n1w = calloc("n1w", 16, F32)
    n2w = calloc("n2w", 16, F32)
    gsb = calloc("gsb", 16 * 8, F32)
    i99 = calloc("i99s", 8, F32)
    onef = calloc("onef", 1, F32)
    oneb = calloc("oneb", 1, BF16)
    epsc = calloc("epsc", 1, F32)
    wcol = calloc("wcol", 8 * 8, F32)
    scl = calloc("scl", 16, F32)
    gsc = calloc("gsc", 48, F32)
    onr = nc.alloc_sbuf_tensor_at("onr", [1, 128], F32, offset=o[0]); o[0] += 512
    assert o[0] <= BASE + 23 * 1024, o[0]

    A1 = BASE + 23 * 1024
    A2 = A1 + 64 * 1024
    A3 = A2 + 64 * 1024
    A4 = A3 + 32 * 1024
    assert A4 + 24 * 1024 <= 229344, A4 + 24 * 1024

    xfm = alloc("xfm", 16 * 1024, BF16, A1)
    vtm = alloc("vtm", 8 * 2048, BF16, A1 + 32768)
    qga = alloc("qga", 16 * 1024, BF16, A1)
    afm = alloc("afm", 16 * 1024, BF16, A1 + 32768)
    cbuf = alloc("cbuf", 8 * 2048, F32, A1)
    wosl = alloc("wosl", 16 * 512, BF16, A1)
    lnsc = alloc("lnsc", 2048, F32, A1 + 16384)
    lnsb = alloc("lnsb", 2 * 1024, BF16, A1 + 24576)
    abuf = alloc("abuf", 1024, F32, A1 + 28672)
    bbuf = alloc("bbuf", 1024, F32, A1 + 36864)
    srow = nc.alloc_sbuf_tensor_at("srow", [1, 4096], F32, offset=A1 + 40960)

    h1 = alloc("h1", 8 * 2048, F32, A2)
    wvsl = alloc("wvsl", 16 * 512, BF16, A2)
    glaW = alloc("glaW", 3 * 16 * 256, BF16, A2)
    glq = alloc("glq", 2 * 1024, BF16, A2 + 24576)
    glk = alloc("glk", 2 * 1024, BF16, A2 + 28672)
    glB = alloc("glB", 2 * 1024, F32, A2 + 32768)
    gls1 = alloc("gls1", 1024, F32, A2 + 40960)
    gls2 = alloc("gls2", 1024, F32, A2 + 45056)
    escr = alloc("escr", 2 * 512, F32, A2 + 49152)
    eout = alloc("eout", 2 * 512, F32, A2 + 53248)
    qhat = alloc("qhat", 2 * 128, BF16, A2 + 57344)
    qhv = alloc("qhv", 2 * 192, BF16, A2 + 57856)
    ktil = alloc("ktil", 2 * 128, BF16, A2 + 58624)
    khat = alloc("khat", 2 * 128, BF16, A2 + 59136)
    sbA = alloc("sbA", 2 * 128, BF16, A2 + 59648)
    sbKT = alloc("sbKT", 2 * 128, BF16, A2 + 60160)
    gvec = alloc("gvec", 16, F32, A2 + 60672)
    xt1f = alloc("xt1f", 16 * 1024, BF16, A2)
    wqsl = alloc("wqsl", 16 * 512, BF16, A2 + 32768)
    rop1 = alloc("rop1", 1024, F32, A2 + 49152)
    rop2 = alloc("rop2", 1024, F32, A2 + 53248)
    Ebuf = alloc("Ebuf", 3 * 1024, BF16, A2)
    den = alloc("den", 1024, F32, A2 + 6144)
    rdb = alloc("rdb", 1024, F32, A2 + 10240)
    rdrow = nc.alloc_sbuf_tensor_at("rdrow", [1, 1024], F32, offset=A2 + 14336)
    moeh = alloc("moeh", 16 * 1024, BF16, A2)
    w2r = alloc("w2r", 16 * 2048, BF16, A2 + 32768)

    ybuf = alloc("ybuf", 16 * 1024, BF16, A3)
    kown = alloc("kown", KV * 1024, BF16, A3)
    vown = alloc("vown", 8 * 512, BF16, A3 + 8192)
    kctx = alloc("kctx", KV * 1024, BF16, A3 + 16384)
    vctx = alloc("vctx", 8 * 512, BF16, A3 + 24576)
    gwos = alloc("gwos", 16 * 512, BF16, A3)
    xt2f = alloc("xt2f", 16 * 1024, BF16, A3)

    ropc = alloc("ropc", 2048, F32, A4)
    rops = alloc("rops", 2048, F32, A4 + 8192)
    p4sq = alloc("p4sq", 2048, F32, A4)
    p4xt = alloc("p4xt", 2048, BF16, A4 + 8192)
    xt2t = alloc("xt2t", 2048, F32, A4)
    fmf = alloc("fmf", 16 * 128, F32, A4 + 8192)
    p7sq = alloc("p7sq", 2048, F32, A4 + 16384)
    wu1s = alloc("wu1s", 2 * 2048, BF16, A4)
    wu3s = alloc("wu3s", 2 * 2048, BF16, A4 + 8192)
    slb = alloc("slb", 1024, BF16, A4 + 16384)

    PS = [nc.alloc_psum_tensor(f"PS{i}", [128, 512], F32) for i in range(6)]
    PSB = nc.alloc_psum_tensor("PSB", [128, 256], BF16)
    psl = [Slot() for _ in range(6)]
    pbs = Slot()

    store_recs = []


    def fm_src(t, rowstride, col0, ncols):
        return bass.AP(t, col0, [[rowstride, 128], [128 * rowstride, 16], [1, ncols]])

    def fm_src3(t, e, rowstride, col0, ncols):
        return bass.AP(t, e * D * rowstride + col0,
                       [[rowstride, 128], [128 * rowstride, 16], [1, ncols]])

    def fm_dst(sb, ncols):
        tot = sb.shape[1]
        return bass.AP(sb, 0, [[tot, 128], [ncols, 16], [1, ncols]])

    def dma(eng, out_ap, in_ap, reads=(), writes=(), deps=(), transpose=False):
        r = kb.op(eng, (lambda h, o_=out_ap, i_=in_ap, t_=transpose:
                        h.dma_start(o_, i_, transpose=t_) if t_ else h.dma_start(o_, i_)),
                  reads=reads, writes=writes, dma=True, deps=deps)
        r.need = True
        return r

    def TAP(tag, half_, buf, isf32, cols):
        if tap != (tag, half_):
            return
        kb.barrier()
        dst = hDbgF if isf32 else hDbgB
        r = dma("sp", dst[:, 0:cols], buf[:, 0:cols])
        store_recs.append(r)
        kb.barrier()

    wbdep = {}
    for nm in ("Wq", "Wk", "Wv", "Wg", "Wo"):
        rs = []
        for j in range(4):
            rs.append(dma("pool", sWB[nm][512 * j:512 * (j + 1), :],
                          hGW[nm][512 * j:512 * (j + 1), :]))
        wbdep[nm] = rs

    for dst, src, tag in ((idb, hIdb, "idb"), (idf, hIdf, "idf"), (swp, hSwp, "swp"),
                          (mA, hMA, "mA"), (em, hEM, "em"), (gam, hGam, "gam"),
                          (bet, hBet, "bet"), (n1w, hN1, "n1w"), (n2w, hN2, "n2w"),
                          (i99, hI99, "i99")):
        dma("sp", dst[:, :], src[:, :], writes=[S("k_" + tag)])
    dma("sp", fm_dst(gsb, 8), fm_src(hGate, 8, 0, 8), writes=[S("k_gsb")])

    kb.op("dve", lambda h: h.memset(S32[:, :], 0.0),
          writes=[S(f"S32_{hh}") for hh in range(H)])
    kb.op("dve", lambda h: h.memset(Sbf[:, :], 0.0),
          writes=[S(f"Sbf_{hh}") for hh in range(H)])
    kb.op("dve", lambda h: h.memset(zf[:, :], 0.0), writes=[S("zf")])
    kb.op("dve", lambda h: h.memset(zb[:, :], 0.0), writes=[S("zb")])
    kb.op("dve", lambda h: h.memset(sbA[:, :], 0.0), writes=[S("sbA0"), S("sbA1")])
    kb.op("dve", lambda h: h.memset(onef[:, :], 1.0), writes=[S("onef")])
    kb.op("dve", lambda h: h.memset(oneb[:, :], 1.0), writes=[S("oneb")])
    kb.op("dve", lambda h: h.memset(epsc[:, :], EPS), writes=[S("epsc")])
    kb.op("dve", lambda h: h.memset(onr[:, :], 1.0), writes=[S("onr")])

    PH = [0]

    def bar():
        if kb.dead:
            return
        kb.barrier()
        PH[0] += 1
        if limit is not None and PH[0] >= limit:
            kb.dead = True

    bar()

    def norm_stats(tile_ap, tile_slot, sqbuf, sq_slot, with_mean):
        kb.op("pool", lambda h, i=tile_ap: h.tensor_mul(sqbuf[:, 0:2048], i, i),
              reads=[tile_slot], writes=[sq_slot])
        kb.op("dve", lambda h: h.reduce_sum(scl[:, 2:3], sqbuf[:, 0:2048], axis=AX.X),
              reads=[sq_slot], writes=[S("sc_s2")])
        kb.op("dve", lambda h: h.tensor_scalar(scl[:, 3:4], scl[:, 2:3], 1.0 / D, None, ALU.mult),
              reads=[S("sc_s2")], writes=[S("sc_msq")])
        if with_mean:
            kb.op("dve", lambda h, i=tile_ap: h.reduce_sum(scl[:, 0:1], i, axis=AX.X),
                  reads=[tile_slot], writes=[S("sc_s1")])
            kb.op("dve", lambda h: h.tensor_scalar(scl[:, 1:2], scl[:, 0:1], 1.0 / D, None, ALU.mult),
                  reads=[S("sc_s1")], writes=[S("sc_mu")])
            kb.op("dve", lambda h: h.tensor_scalar(scl[:, 4:5], scl[:, 1:2], scl[:, 1:2], None, ALU.mult),
                  reads=[S("sc_mu")], writes=[S("sc_mu2")])
            kb.op("dve", lambda h: h.tensor_scalar(scl[:, 5:6], scl[:, 3:4], scl[:, 4:5], None, ALU.subtract),
                  reads=[S("sc_msq"), S("sc_mu2")], writes=[S("sc_var")])
            var_ap, var_slot = scl[:, 5:6], S("sc_var")
        else:
            var_ap, var_slot = scl[:, 3:4], S("sc_msq")
        kb.op("act", lambda h, i=var_ap: h.activation(scl[:, 8:9], i, AF.Ln, bias=epsc[:, 0:1]),
              reads=[var_slot, S("epsc")], writes=[S("sc_lnt")])
        kb.op("act", lambda h: h.activation(scl[:, 6:7], scl[:, 8:9], AF.Exp, scale=-0.5),
              reads=[S("sc_lnt")], writes=[S("sc_rstd")])
        mu = (scl[:, 1:2], S("sc_mu")) if with_mean else (None, None)
        return mu, (scl[:, 6:7], S("sc_rstd"))

    for half in range(2):
        hb = half * TH

        xb_sl = [S("sXb0"), S("sXb1")]
        for j in range(2):
            dma("pool", sXb[512 * j:512 * (j + 1), :], hX[hb + 512 * j:hb + 512 * (j + 1), :],
                writes=[xb_sl[j]])
        xfm_sl = [S(f"xfm_{dc}") for dc in range(16)]
        for dc in range(16):
            dma("sp", xfm[:, 1024 * dc:1024 * (dc + 1)], sXb[:, 128 * dc:128 * (dc + 1)],
                reads=xb_sl, writes=[xfm_sl[dc]], transpose=True)

        TAP("xfm", half, xfm, False, 16384)
        for vg in range(4):
            wv_sl = [S(f"wvsl_{dc}") for dc in range(16)]
            for dc in range(16):
                dma("sp", wvsl[:, 512 * dc:512 * (dc + 1)],
                    sWB["Wv"][512 * vg:512 * (vg + 1), 128 * dc:128 * (dc + 1)],
                    writes=[wv_sl[dc]], deps=wbdep["Wv"], transpose=True)
            for tt in range(8):
                ps = PS[tt % 2]

                def vchain(h, ps_=ps, tt_=tt):
                    inst = None
                    for dc in range(16):
                        inst = h.matmul(ps_[:, :],
                                        xfm[:, 1024 * dc + 128 * tt_:1024 * dc + 128 * (tt_ + 1)],
                                        wvsl[:, 512 * dc:512 * (dc + 1)],
                                        start=(dc == 0), stop=(dc == 15))
                    return inst
                kb.op("pe", vchain, reads=xfm_sl + wv_sl, writes=[psl[tt % 2]])
                kb.op("dve", lambda h, ps_=ps, t_=tt, v_=vg:
                      h.tensor_copy(vtm[:, 2048 * t_ + 512 * v_:2048 * t_ + 512 * (v_ + 1)],
                                    ps_[:, :]),
                      reads=[psl[tt % 2]], writes=[S(f"vtm_{tt}_{vg}")])
        TAP("vtm", half, vtm, False, 16384)
        bar()
        xfm_sl = [S(f"xfm_{dc}") for dc in range(16)]

        kb.op("dve", lambda h: h.memset(sbA[:, :], 0.0), writes=[S("sbA0"), S("sbA1")])
        for pair in range(p2_pairs):
            wsl = {}
            for wi, nm in enumerate(("Wq", "Wk", "Wg")):
                ss = [S(f"gW{wi}_{dc}") for dc in range(16)]
                for dc in range(16):
                    dma("sp", glaW[:, 4096 * wi + 256 * dc:4096 * wi + 256 * (dc + 1)],
                        sWB[nm][256 * pair:256 * (pair + 1), 128 * dc:128 * (dc + 1)],
                        writes=[ss[dc]], deps=wbdep[nm], transpose=True)
                wsl[nm] = ss

            for hh in range(2):
                if not (p2_mask & 1):
                    break
                for wi, nm, dstbuf, dtag in ((0, "Wq", glq, "glq"), (1, "Wk", glk, "glk")):
                    for t5 in range(2):
                        ps = PS[2 + t5]

                        def pchain(h, ps_=ps, wi_=wi, hh_=hh, t5_=t5):
                            inst = None
                            for dc in range(16):
                                inst = h.matmul(
                                    ps_[:, :],
                                    glaW[:, 4096 * wi_ + 256 * dc + 128 * hh_:
                                         4096 * wi_ + 256 * dc + 128 * hh_ + 128],
                                    xfm[:, 1024 * dc + 512 * t5_:1024 * dc + 512 * (t5_ + 1)],
                                    start=(dc == 0), stop=(dc == 15))
                            return inst
                        kb.op("pe", pchain, reads=wsl[nm] + xfm_sl, writes=[psl[2 + t5]])
                        kb.op("dve", lambda h, ps_=ps, b=dstbuf, hh_=hh, t5_=t5:
                              h.tensor_copy(b[:, 1024 * hh_ + 512 * t5_:1024 * hh_ + 512 * (t5_ + 1)],
                                            ps_[:, :]),
                              reads=[psl[2 + t5]], writes=[S(f"{dtag}_{hh}")])
                for t5 in range(2 if (p2_mask & 2) else 0):
                    ps = PS[4]
                    c5 = slice(512 * t5, 512 * (t5 + 1))

                    def gchain(h, ps_=ps, hh_=hh, t5_=t5):
                        inst = None
                        for dc in range(16):
                            inst = h.matmul(
                                ps_[:, :],
                                glaW[:, 8192 + 256 * dc + 128 * hh_:8192 + 256 * dc + 128 * hh_ + 128],
                                xfm[:, 1024 * dc + 512 * t5_:1024 * dc + 512 * (t5_ + 1)],
                                start=(dc == 0), stop=(dc == 15))
                        return inst
                    kb.op("pe", gchain, reads=wsl["Wg"] + xfm_sl, writes=[psl[4]])
                    kb.op("dve", lambda h, ps_=ps, c=c5:
                          h.tensor_copy(gls2[:, c], ps_[:, :]),
                          reads=[psl[4]], writes=[S("gz")])
                    kb.op("act", lambda h, c=c5:
                          h.activation(gls1[:, c], gls2[:, c], AF.Abs),
                          reads=[S("gz")], writes=[S("gaz")])
                    kb.op("act", lambda h, c=c5: h.activation(gls1[:, c], gls1[:, c], AF.Exp,
                                                              scale=-1.0),
                          reads=[S("gaz")], writes=[S("gaz")])
                    kb.op("act", lambda h, c=c5: h.activation(gls1[:, c], gls1[:, c], AF.Ln,
                                                              bias=onef[:, 0:1]),
                          reads=[S("gaz"), S("onef")], writes=[S("gaz")])
                    kb.op("dve", lambda h, c=c5:
                          h.tensor_scalar(gls2[:, c], gls2[:, c], 0.0, None, ALU.min),
                          reads=[S("gz")], writes=[S("gz")])
                    kb.op("dve", lambda h, c=c5: h.tensor_sub(gls2[:, c], gls2[:, c], gls1[:, c]),
                          reads=[S("gz"), S("gaz")], writes=[S(f"gg_{t5}")])
                if p2_mask & 4:
                    kb.op("dve", lambda h, hh_=hh:
                          h.tensor_tensor_scan(glB[:, 1024 * hh_:1024 * (hh_ + 1)], gls2[:, 0:1024],
                                               zf[:, :], 0.0, ALU.add, ALU.add),
                          reads=[S("gg_0"), S("gg_1"), S("zf")], writes=[S(f"glB_{hh}")])

            nqoff = [0, 256, 352, 416]
            for T in range(p2_chunks):
                for hh in range(2):
                    head = pair * 2 + hh
                    ct = 1024 * hh + 128 * T
                    rot = (T * 2 + hh) % 2
                    ro = 128 * rot
                    e_o = 512 * rot
                    Bsl = S(f"glB_{hh}")
                    qsl = S(f"glq_{hh}")
                    ksl = S(f"glk_{hh}")
                    esl = S(f"esc{rot}")
                    eol = S(f"eo{rot}")

                    if T > 0:
                        kb.op("dve", lambda h, c=ct, eo=e_o:
                              h.tensor_scalar(escr[:, eo:eo + 128], glB[:, c:c + 128],
                                              glB[:, c - 1:c], None, ALU.subtract),
                              reads=[Bsl], writes=[esl])
                        kb.op("dve", lambda h, c=ct, eo=e_o:
                              h.tensor_scalar(escr[:, eo + 448:eo + 449], glB[:, c + 127:c + 128],
                                              glB[:, c - 1:c], None, ALU.subtract),
                              reads=[Bsl], writes=[esl])
                    else:
                        kb.op("dve", lambda h, c=ct, eo=e_o:
                              h.tensor_copy(escr[:, eo:eo + 128], glB[:, c:c + 128]),
                              reads=[Bsl], writes=[esl])
                        kb.op("dve", lambda h, c=ct, eo=e_o:
                              h.tensor_copy(escr[:, eo + 448:eo + 449], glB[:, c + 127:c + 128]),
                              reads=[Bsl], writes=[esl])
                    for J in range(4):
                        if J == 0 and T == 0:
                            kb.op("dve", lambda h, c=ct, eo=e_o:
                                  h.tensor_scalar(escr[:, eo + 128:eo + 160], glB[:, c:c + 32],
                                                  -1.0, None, ALU.mult),
                                  reads=[Bsl], writes=[esl])
                        else:
                            bj = ct - 1 if J == 0 else ct + 32 * J - 1
                            kb.op("dve", lambda h, c=ct, eo=e_o, J_=J, b=bj:
                                  h.tensor_scalar(escr[:, eo + 128 + 32 * J_:eo + 160 + 32 * J_],
                                                  glB[:, c + 32 * J_:c + 32 * (J_ + 1)],
                                                  glB[:, b:b + 1], -1.0, ALU.subtract, ALU.mult),
                                  reads=[Bsl], writes=[esl])
                    for J in range(1, 4):
                        kb.op("dve", lambda h, c=ct, eo=e_o, J_=J:
                              h.tensor_scalar(escr[:, eo + nqoff[J_]:eo + nqoff[J_] + 128 - 32 * J_],
                                              glB[:, c + 32 * J_:c + 128],
                                              glB[:, c + 32 * J_ - 1:c + 32 * J_], None,
                                              ALU.subtract),
                              reads=[Bsl], writes=[esl])
                    for J in range(1, 4):
                        kb.op("dve", lambda h, c=ct, eo=e_o, J_=J:
                              h.tensor_scalar(escr[:, eo + 448 + J_:eo + 449 + J_],
                                              glB[:, c + 32 * J_ - 1:c + 32 * J_],
                                              glB[:, c + 127:c + 128], -1.0, ALU.subtract,
                                              ALU.mult),
                              reads=[Bsl], writes=[esl])
                    kb.op("act", lambda h, eo=e_o:
                          h.activation(eout[:, eo:eo + 452], escr[:, eo:eo + 452], AF.Exp),
                          reads=[esl], writes=[eol])
                    kb.op("dve", lambda h, c=ct, eo=e_o, r=ro:
                          h.scalar_tensor_tensor(qhat[:, r:r + 128], glq[:, c:c + 128], RSQ,
                                                 eout[:, eo:eo + 128], ALU.mult, ALU.mult),
                          reads=[qsl, eol], writes=[S(f"qhat{rot}")])
                    for J in range(1, 4):
                        kb.op("dve", lambda h, c=ct, eo=e_o, J_=J, r=192 * rot:
                              h.scalar_tensor_tensor(
                                  qhv[:, r + nqoff[J_] - 256:r + nqoff[J_] - 256 + 128 - 32 * J_],
                                  glq[:, c + 32 * J_:c + 128], RSQ,
                                  eout[:, eo + nqoff[J_]:eo + nqoff[J_] + 128 - 32 * J_],
                                  ALU.mult, ALU.mult),
                              reads=[qsl, eol], writes=[S(f"qhv{rot}")])
                    kb.op("dve", lambda h, c=ct, eo=e_o, r=ro:
                          h.tensor_mul(ktil[:, r:r + 128], glk[:, c:c + 128],
                                       eout[:, eo + 128:eo + 256]),
                          reads=[ksl, eol], writes=[S(f"ktil{rot}")])
                    for J in range(4):
                        kb.op("dve", lambda h, r=ro, eo=e_o, J_=J:
                              h.tensor_scalar(khat[:, r + 32 * J_:r + 32 * (J_ + 1)],
                                              ktil[:, r + 32 * J_:r + 32 * (J_ + 1)],
                                              eout[:, eo + 448 + J_:eo + 449 + J_], None,
                                              ALU.mult),
                              reads=[S(f"ktil{rot}"), eol], writes=[S(f"khat{rot}")])
                    kb.op("act", lambda h, eo=e_o, r=rot:
                          h.activation(gvec[:, 8 * r:8 * r + 1], escr[:, eo + 448:eo + 449],
                                       AF.Exp),
                          reads=[esl], writes=[S(f"Gam{rot}")])

                    psA = PS[5]

                    def ablocks(h, r=ro, rr=192 * rot):
                        inst = None
                        for I in range(4):
                            for J in range(I + 1):
                                lt = ktil[:, r + 32 * J:r + 32 * (J + 1)]
                                if J == 0:
                                    rh = qhat[:, r + 32 * I:r + 32 * (I + 1)]
                                else:
                                    b0 = rr + nqoff[J] - 256 + 32 * (I - J)
                                    rh = qhv[:, b0:b0 + 32]
                                inst = h.matmul(psA[32 * J:32 * J + 32, 32 * I:32 * I + 32],
                                                lt, rh, start=True, stop=True,
                                                tile_position=(0, 32 * J))
                        return inst
                    kb.op("pe", ablocks,
                          reads=[S(f"qhat{rot}"), S(f"qhv{rot}"), S(f"ktil{rot}")],
                          writes=[psl[5]])
                    kb.op("dve", lambda h, r=ro:
                          h.copy_predicated(sbA[:, r:r + 128], mA[:, :], psA[:, 0:128]),
                          reads=[psl[5]], writes=[S(f"sbA{rot}")])
                    if pair == 0 and hh == 0 and T == 0:
                        TAP("sbA00", half, sbA, False, 256)
                        TAP("qhat00", half, qhat, False, 256)
                        TAP("qhv00", half, qhv, False, 384)
                        TAP("ktil00", half, ktil, False, 256)

                    vt_sl = [S(f"vtm_{T}_{vg}") for vg in range(4)]

                    def yfn(h, r=ro, rt=rot, hd=head, T_=T):
                        if p2_mask & 256:
                            return h.matmul(PS[rt][:, 0:128],
                                            vtm[:, 2048 * T_ + 128 * hd:2048 * T_ + 128 * (hd + 1)],
                                            sbA[:, r:r + 128], start=True, stop=True)
                        h.matmul(PS[rt][:, 0:128],
                                 vtm[:, 2048 * T_ + 128 * hd:2048 * T_ + 128 * (hd + 1)],
                                 sbA[:, r:r + 128], start=True, stop=False)
                        return h.matmul(PS[rt][:, 0:128],
                                        Sbf[:, 128 * hd:128 * (hd + 1)],
                                        qhat[:, r:r + 128], start=False, stop=True)
                    ymm = kb.op("pe", yfn,
                                reads=[S(f"sbA{rot}"), S(f"qhat{rot}"), S(f"Sbf_{head}")] + vt_sl,
                                writes=[psl[rot]])
                    kb.op("dve", lambda h, hd=head, T_=T, rt=rot:
                          h.tensor_copy(ybuf[:, 1024 * hd + 128 * T_:1024 * hd + 128 * (T_ + 1)],
                                        PS[rt][:, 0:128]),
                          reads=[psl[rot]], writes=[S(f"y_{head}_{T}")])

                    kb.op("pe", lambda h, r=ro: h.transpose(PSB[:, 128:256], khat[:, r:r + 128],
                                                            idb[:, :]),
                          reads=[S(f"khat{rot}")], writes=[pbs])
                    kb.op("dve", lambda h, r=ro: h.tensor_copy(sbKT[:, r:r + 128], PSB[:, 128:256]),
                          reads=[pbs], writes=[S(f"sbKT{rot}")])
                    kb.op("pe", lambda h, r=ro, rt=rot, hd=head, T_=T:
                          h.matmul(PS[2 + rt][:, 0:128], sbKT[:, r:r + 128],
                                   vtm[:, 2048 * T_ + 128 * hd:2048 * T_ + 128 * (hd + 1)],
                                   start=True, stop=True),
                          reads=[S(f"sbKT{rot}")] + vt_sl, writes=[psl[2 + rot]])
                    kb.op("dve", lambda h, hd=head, rt=rot:
                          h.scalar_tensor_tensor(S32[:, 128 * hd:128 * (hd + 1)],
                                                 S32[:, 128 * hd:128 * (hd + 1)],
                                                 gvec[:, 8 * rt:8 * rt + 1],
                                                 PS[2 + rt][:, 0:128], ALU.mult, ALU.add),
                          reads=[psl[2 + rot], S(f"Gam{rot}"), S(f"S32_{head}")],
                          writes=[S(f"S32_{head}")])
                    kb.op("dve", lambda h, hd=head:
                          h.tensor_copy(Sbf[:, 128 * hd:128 * (hd + 1)],
                                        S32[:, 128 * hd:128 * (hd + 1)]),
                          reads=[S(f"S32_{head}")], writes=[S(f"Sbf_{head}")],
                          deps=[ymm])
        TAP("escrT", half, escr, True, 1024)
        TAP("eoutT", half, eout, True, 1024)
        TAP("ktilT", half, ktil, False, 256)
        TAP("khatT", half, khat, False, 256)
        TAP("sbKTT", half, sbKT, False, 256)
        TAP("qhatT", half, qhat, False, 256)
        TAP("gvecT", half, gvec, True, 16)
        TAP("y", half, ybuf, False, 16384)
        TAP("S32", half, S32, True, 2048)
        TAP("Sbf", half, Sbf, False, 2048)
        TAP("glB", half, glB, True, 2048)
        TAP("glq", half, glq, False, 2048)
        TAP("glk", half, glk, False, 2048)
        bar()

        y3 = [S(f"y3_{hc}") for hc in range(16)]
        for hc in range(16):
            sqs = S(f"ysq{hc % 2}")
            kb.op("pool", lambda h, c=hc, b=1024 * (hc % 2):
                  h.tensor_mul(lnsb[:, b:b + 1024], ybuf[:, 1024 * c:1024 * (c + 1)],
                               ybuf[:, 1024 * c:1024 * (c + 1)]),
                  reads=[y3[hc]], writes=[sqs])
            for t5 in range(2):
                kb.op("pe", lambda h, c=hc, t_=t5:
                      h.matmul(PS[t_][0:1, :], oneb[:, 0:1],
                               ybuf[:, 1024 * c + 512 * t_:1024 * c + 512 * (t_ + 1)],
                               start=(c == 0), stop=(c == 15)),
                      reads=[y3[hc]], writes=[psl[t5]])
                kb.op("pe", lambda h, c=hc, t_=t5, b=1024 * (hc % 2):
                      h.matmul(PS[2 + t_][0:1, :], oneb[:, 0:1],
                               lnsb[:, b + 512 * t_:b + 512 * (t_ + 1)],
                               start=(c == 0), stop=(c == 15)),
                      reads=[sqs], writes=[psl[2 + t5]])
        for t5 in range(2):
            kb.op("dve", lambda h, t_=t5:
                  h.tensor_scalar(srow[0:1, 512 * t_:512 * (t_ + 1)], PS[t_][0:1, :],
                                  1.0 / D, None, ALU.mult),
                  reads=[psl[t5]], writes=[S("r_mu")])
            kb.op("dve", lambda h, t_=t5:
                  h.tensor_scalar(srow[0:1, 3072 + 512 * t_:3072 + 512 * (t_ + 1)],
                                  PS[2 + t_][0:1, :], 1.0 / D, None, ALU.mult),
                  reads=[psl[2 + t5]], writes=[S("r_msq")])
        kb.op("dve", lambda h: h.tensor_mul(srow[0:1, 1024:2048], srow[0:1, 0:1024],
                                            srow[0:1, 0:1024]),
              reads=[S("r_mu")], writes=[S("r_mu2")])
        kb.op("dve", lambda h: h.tensor_sub(srow[0:1, 3072:4096], srow[0:1, 3072:4096],
                                            srow[0:1, 1024:2048]),
              reads=[S("r_msq"), S("r_mu2")], writes=[S("r_var")])
        kb.op("act", lambda h: h.activation(srow[0:1, 1024:2048], srow[0:1, 3072:4096], AF.Ln,
                                            bias=epsc[0:1, 0:1]),
              reads=[S("r_var"), S("epsc")], writes=[S("r_a")])
        kb.op("act", lambda h: h.activation(srow[0:1, 1024:2048], srow[0:1, 1024:2048], AF.Exp,
                                            scale=-0.5),
              reads=[S("r_a")], writes=[S("r_a")])
        kb.op("dve", lambda h: h.tensor_mul(srow[0:1, 2048:3072], srow[0:1, 0:1024],
                                            srow[0:1, 1024:2048]),
              reads=[S("r_a"), S("r_mu")], writes=[S("r_b")])
        kb.op("dve", lambda h: h.tensor_scalar(srow[0:1, 2048:3072], srow[0:1, 2048:3072],
                                               -1.0, None, ALU.mult),
              reads=[S("r_b")], writes=[S("r_b")])
        for t5 in range(2):
            kb.op("pe", lambda h, t_=t5:
                  h.matmul(PS[4][:, :], onr[0:1, :],
                           srow[0:1, 1024 + 512 * t_:1024 + 512 * (t_ + 1)],
                           start=True, stop=True),
                  reads=[S("r_a"), S("onr")], writes=[psl[4]])
            kb.op("dve", lambda h, t_=t5:
                  h.tensor_copy(abuf[:, 512 * t_:512 * (t_ + 1)], PS[4][:, :]),
                  reads=[psl[4]], writes=[S("r_ab")])
            kb.op("pe", lambda h, t_=t5:
                  h.matmul(PS[5][:, :], onr[0:1, :],
                           srow[0:1, 2048 + 512 * t_:2048 + 512 * (t_ + 1)],
                           start=True, stop=True),
                  reads=[S("r_b"), S("onr")], writes=[psl[5]])
            kb.op("dve", lambda h, t_=t5:
                  h.tensor_copy(bbuf[:, 512 * t_:512 * (t_ + 1)], PS[5][:, :]),
                  reads=[psl[5]], writes=[S("r_bb")])
        lny_sl = [S(f"lny_{hc}") for hc in range(16)]
        for hc in range(16):
            kb.op("dve", lambda h, c=hc:
                  h.tensor_mul(lnsc[:, 0:1024], ybuf[:, 1024 * c:1024 * (c + 1)], abuf[:, :]),
                  reads=[S("r_ab"), y3[hc]], writes=[S("lnt")])
            kb.op("dve", lambda h: h.tensor_add(lnsc[:, 0:1024], lnsc[:, 0:1024], bbuf[:, :]),
                  reads=[S("lnt"), S("r_bb")], writes=[S("lnt")])
            kb.op("dve", lambda h, c=hc:
                  h.tensor_scalar(ybuf[:, 1024 * c:1024 * (c + 1)], lnsc[:, 0:1024],
                                  gam[:, c:c + 1], bet[:, c:c + 1], ALU.mult, ALU.add),
                  reads=[S("lnt")], writes=[y3[hc], lny_sl[hc]])
        h1_sl = [S(f"h1_{tt}") for tt in range(8)]
        for tt in range(8):
            dma("sp", h1[:, 2048 * tt:2048 * (tt + 1)], hX[hb + 128 * tt:hb + 128 * (tt + 1), :],
                writes=[h1_sl[tt]])
        for g4 in range(4):
            wo_sl = [S(f"wosl_{dc}") for dc in range(16)]
            for dc in range(16):
                dma("sp", wosl[:, 512 * dc:512 * (dc + 1)],
                    sWB["Wo"][512 * g4:512 * (g4 + 1), 128 * dc:128 * (dc + 1)],
                    writes=[wo_sl[dc]], deps=wbdep["Wo"], transpose=True)
            for tt in range(8):
                ps = PS[tt % 2]

                def ochain(h, ps_=ps, tt_=tt):
                    inst = None
                    for dc in range(16):
                        inst = h.matmul(ps_[:, :],
                                        ybuf[:, 1024 * dc + 128 * tt_:1024 * dc + 128 * (tt_ + 1)],
                                        wosl[:, 512 * dc:512 * (dc + 1)],
                                        start=(dc == 0), stop=(dc == 15))
                    return inst
                kb.op("pe", ochain, reads=lny_sl + wo_sl, writes=[psl[tt % 2]])
                kb.op("dve", lambda h, ps_=ps, tt_=tt, g_=g4:
                      h.tensor_add(h1[:, 2048 * tt_ + 512 * g_:2048 * tt_ + 512 * (g_ + 1)],
                                   h1[:, 2048 * tt_ + 512 * g_:2048 * tt_ + 512 * (g_ + 1)],
                                   ps_[:, :]),
                      reads=[psl[tt % 2], h1_sl[tt]], writes=[h1_sl[tt]])
        for tt in range(8):
            tile = h1[:, 2048 * tt:2048 * (tt + 1)]
            (mu_ap, mu_sl), (rs_ap, rs_sl) = norm_stats(tile, h1_sl[tt], lnsc, S("lnt"), True)
            kb.op("dve", lambda h, t_=tile, m=mu_ap, r=rs_ap:
                  h.tensor_scalar(t_, t_, m, r, ALU.subtract, ALU.mult),
                  reads=[mu_sl, rs_sl, h1_sl[tt]], writes=[h1_sl[tt]])
        TAP("lny", half, ybuf, False, 16384)
        TAP("h1", half, h1, True, 16384)
        bar()

        h1_sl = [S(f"h1_{tt}") for tt in range(8)]
        xt1s_sl = [S(f"xt1s_{tt}") for tt in range(8)]
        for tt in range(8):
            tile = h1[:, 2048 * tt:2048 * (tt + 1)]
            _, (rs_ap, rs_sl) = norm_stats(tile, h1_sl[tt], p4sq, S("p4sq"), False)
            kb.op("dve", lambda h, t_=tile, r=rs_ap:
                  h.tensor_scalar(p4xt[:, 0:2048], t_, r, None, ALU.mult),
                  reads=[rs_sl, h1_sl[tt]], writes=[S("p4xt")])
            dma("sp", sXt1[128 * tt:128 * (tt + 1), :], p4xt[:, 0:2048],
                reads=[S("p4xt")], writes=[xt1s_sl[tt]])
            dma("act", sH1[128 * tt:128 * (tt + 1), :], tile,
                reads=[h1_sl[tt]], writes=[S(f"h1p_{tt}")])
        bar()

        dma("act", ropc[:, :], hRc[:, :], writes=[S("ropc")])
        dma("act", rops[:, :], hRs[:, :], writes=[S("rops")])
        xt1_sl = [S(f"xt1f_{dc}") for dc in range(16)]
        for dc in range(16):
            dma("sp", xt1f[:, 1024 * dc:1024 * (dc + 1)], sXt1[:, 128 * dc:128 * (dc + 1)],
                writes=[xt1_sl[dc]], transpose=True)
            kb.op("dve", lambda h, c=dc:
                  h.tensor_scalar(xt1f[:, 1024 * c:1024 * (c + 1)],
                                  xt1f[:, 1024 * c:1024 * (c + 1)],
                                  n1w[:, c:c + 1], None, ALU.mult),
                  reads=[xt1_sl[dc]], writes=[xt1_sl[dc]])
        wq_s = S("wqsl")
        for g4 in range(4):
            dma("pool", fm_dst(wqsl, 512), fm_src(hFq, D, 512 * g4, 512), writes=[wq_s])
            for hh in range(4):
                head = g4 * 4 + hh
                for t5 in range(2):
                    ps = PS[t5]

                    def qchain(h, hh_=hh, t5_=t5, ps_=ps):
                        inst = None
                        for dc in range(16):
                            inst = h.matmul(ps_[:, :],
                                            wqsl[:, 512 * dc + 128 * hh_:512 * dc + 128 * hh_ + 128],
                                            xt1f[:, 1024 * dc + 512 * t5_:1024 * dc + 512 * (t5_ + 1)],
                                            start=(dc == 0), stop=(dc == 15))
                        return inst
                    kb.op("pe", qchain, reads=[wq_s] + xt1_sl, writes=[psl[t5]])
                    kb.op("dve", lambda h, ps_=ps, hd=head, t5_=t5:
                          h.tensor_copy(qga[:, 1024 * hd + 512 * t5_:1024 * hd + 512 * (t5_ + 1)],
                                        ps_[:, :]),
                          reads=[psl[t5]], writes=[S(f"qga_{head}")])
        dma("pool", fm_dst(wqsl, 512), fm_src(hFk, 512, 0, 512), writes=[wq_s])
        for g in range(KV):
            for t5 in range(2):
                ps = PS[2 + t5]

                def kchain(h, g_=g, t5_=t5, ps_=ps):
                    inst = None
                    for dc in range(16):
                        inst = h.matmul(ps_[:, :],
                                        wqsl[:, 512 * dc + 128 * g_:512 * dc + 128 * g_ + 128],
                                        xt1f[:, 1024 * dc + 512 * t5_:1024 * dc + 512 * (t5_ + 1)],
                                        start=(dc == 0), stop=(dc == 15))
                    return inst
                kb.op("pe", kchain, reads=[wq_s] + xt1_sl, writes=[psl[2 + t5]])
                kb.op("dve", lambda h, ps_=ps, g_=g, t5_=t5:
                      h.tensor_copy(kown[:, 1024 * g_ + 512 * t5_:1024 * g_ + 512 * (t5_ + 1)],
                                    ps_[:, :]),
                      reads=[psl[2 + t5]], writes=[S(f"kown_{g}")])
        dma("pool", fm_dst(wqsl, 512), fm_src(hFv, 512, 0, 512), writes=[wq_s])
        for tt in range(8):
            ps = PS[4 + tt % 2]

            def vchain2(h, tt_=tt, ps_=ps):
                inst = None
                for dc in range(16):
                    inst = h.matmul(ps_[:, :],
                                    xt1f[:, 1024 * dc + 128 * tt_:1024 * dc + 128 * (tt_ + 1)],
                                    wqsl[:, 512 * dc:512 * (dc + 1)],
                                    start=(dc == 0), stop=(dc == 15))
                return inst
            kb.op("pe", vchain2, reads=[wq_s] + xt1_sl, writes=[psl[4 + tt % 2]])
            kb.op("dve", lambda h, ps_=ps, tt_=tt:
                  h.tensor_copy(vown[:, 512 * tt_:512 * (tt_ + 1)], ps_[:, :]),
                  reads=[psl[4 + tt % 2]], writes=[S(f"vown_{tt}")])

        def rope(buf, hd, slot):
            for t5 in range(2):
                cs = slice(1024 * hd + 512 * t5, 1024 * hd + 512 * (t5 + 1))
                gcs = slice(hb + 512 * t5, hb + 512 * (t5 + 1))
                kb.op("pe", lambda h, b=buf, c=cs:
                      h.matmul(PS[5][:, :], swp[:, :], b[:, c], start=True, stop=True),
                      reads=[slot], writes=[psl[5]])
                kb.op("dve", lambda h, g=gcs: h.tensor_mul(rop2[:, 0:512], PS[5][:, :],
                                                           rops[:, g]),
                      reads=[psl[5], S("rops")], writes=[S("rp2")])
                kb.op("dve", lambda h, b=buf, c=cs, g=gcs:
                      h.tensor_mul(rop1[:, 0:512], b[:, c], ropc[:, g]),
                      reads=[slot, S("ropc")], writes=[S("rp1")])
                kb.op("dve", lambda h, b=buf, c=cs:
                      h.tensor_add(b[:, c], rop1[:, 0:512], rop2[:, 0:512]),
                      reads=[S("rp1"), S("rp2")], writes=[slot])
        for hd in range(H):
            rope(qga, hd, S(f"qga_{hd}"))
        for g in range(KV):
            rope(kown, g, S(f"kown_{g}"))
        if half == 0:
            dma("act", sKc[:, :], kown[:, :], reads=[S(f"kown_{g}") for g in range(KV)],
                writes=[S("sKc")])
            dma("act", sVc[:, :], vown[:, :], reads=[S(f"vown_{t}") for t in range(8)],
                writes=[S("sVc")])
        else:
            dma("act", kctx[:, :], sKc[:, :], writes=[S("kctx")])
            dma("act", vctx[:, :], sVc[:, :], writes=[S("vctx")])
        TAP("xt1f", half, xt1f, False, 16384)
        TAP("qga", half, qga, False, 16384)
        TAP("kown", half, kown, False, 4096)
        TAP("vown", half, vown, False, 4096)
        bar()

        schunks = []
        if half == 1:
            schunks += [(kctx, vctx, i, False) for i in range(8)]
        schunks += [(kown, vown, i, True) for i in range(8)]
        nch = len(schunks)
        for hd in range(H):
            g = hd // 4
            qs = S(f"qga_{hd}")
            den_sl = S("den")
            for ci, (kbuf, vbuf, i, own) in enumerate(schunks):
                er = ci % 3
                esl = S(f"E{er}")
                kslot = S(f"kown_{g}") if own else S("kctx")
                vslot = S(f"vown_{i}") if own else S("vctx")
                for t5 in range(2):
                    pp = ci % 2
                    kb.op("pe", lambda h, kb_=kbuf, g_=g, i_=i, hd_=hd, t_=t5, p_=pp:
                          h.matmul(PS[p_][:, :],
                                   kb_[:, 1024 * g_ + 128 * i_:1024 * g_ + 128 * (i_ + 1)],
                                   qga[:, 1024 * hd_ + 512 * t_:1024 * hd_ + 512 * (t_ + 1)],
                                   start=True, stop=True),
                          reads=[kslot, qs], writes=[psl[pp]])
                    kb.op("act", lambda h, er_=er, t_=t5, p_=pp:
                          h.activation(Ebuf[:, 1024 * er_ + 512 * t_:1024 * er_ + 512 * (t_ + 1)],
                                       PS[p_][:, :], AF.Exp, scale=RSQ),
                          reads=[psl[pp]], writes=[esl])
                if own:
                    w_ = 128 * (i + 1)
                    kb.op("dve", lambda h, er_=er, i_=i, w__=w_:
                          h.copy_predicated(Ebuf[:, 1024 * er_:1024 * er_ + w__],
                                            em[:, 1024 - 128 * i_:1024 - 128 * i_ + w__],
                                            zb[:, 0:w__]),
                          reads=[esl], writes=[esl])
                if ci == 0:
                    kb.op("pool", lambda h, er_=er:
                          h.tensor_copy(den[:, :], Ebuf[:, 1024 * er_:1024 * (er_ + 1)]),
                          reads=[esl], writes=[den_sl])
                else:
                    kb.op("pool", lambda h, er_=er:
                          h.tensor_add(den[:, :], den[:, :], Ebuf[:, 1024 * er_:1024 * (er_ + 1)]),
                          reads=[esl, den_sl], writes=[den_sl])
                for t5 in range(2):
                    kb.op("pe", lambda h, vb=vbuf, i_=i, g_=g, er_=er, t_=t5,
                          st=(ci == 0), sp_=(ci == nch - 1):
                          h.matmul(PS[2 + t_][:, :],
                                   vb[:, 512 * i_ + 128 * g_:512 * i_ + 128 * (g_ + 1)],
                                   Ebuf[:, 1024 * er_ + 512 * t_:1024 * er_ + 512 * (t_ + 1)],
                                   start=st, stop=sp_),
                          reads=[esl, vslot], writes=[psl[2 + t5]])
            for t5 in range(2):
                kb.op("pe", lambda h, t_=t5:
                      h.matmul(PS[4 + t_][0:1, :], onef[:, 0:1], den[:, 512 * t_:512 * (t_ + 1)],
                               start=True, stop=True),
                      reads=[den_sl], writes=[psl[4 + t5]])
                kb.op("dve", lambda h, t_=t5:
                      h.reciprocal(rdrow[0:1, 512 * t_:512 * (t_ + 1)], PS[4 + t_][0:1, :]),
                      reads=[psl[4 + t5]], writes=[S("rdrow")])
            for t5 in range(2):
                kb.op("pe", lambda h, t_=t5:
                      h.matmul(PS[4 + t_][:, :], onr[0:1, :],
                               rdrow[0:1, 512 * t_:512 * (t_ + 1)], start=True, stop=True),
                      reads=[S("rdrow"), S("onr")], writes=[psl[4 + t5]])
                kb.op("dve", lambda h, t_=t5:
                      h.tensor_copy(rdb[:, 512 * t_:512 * (t_ + 1)], PS[4 + t_][:, :]),
                      reads=[psl[4 + t5]], writes=[S("rdb")])
            for t5 in range(2):
                kb.op("dve", lambda h, hd_=hd, t_=t5:
                      h.tensor_mul(afm[:, 1024 * hd_ + 512 * t_:1024 * hd_ + 512 * (t_ + 1)],
                                   PS[2 + t_][:, :], rdb[:, 512 * t_:512 * (t_ + 1)]),
                      reads=[psl[2 + t5], S("rdb")], writes=[S(f"afm_{hd}")])
        TAP("afm", half, afm, False, 16384)
        bar()

        hf_sl = [S(f"hf_{tt}") for tt in range(8)]
        for tt in range(8):
            dma("sp", h1[:, 2048 * tt:2048 * (tt + 1)], sH1[128 * tt:128 * (tt + 1), :],
                writes=[hf_sl[tt]])
        ws = S("gwos")
        for g4 in range(4):
            dma("pool", fm_dst(gwos, 512), fm_src(hFo, D, 512 * g4, 512), writes=[ws])
            for tt in range(8):
                ps = PS[tt % 2]

                def o2chain(h, tt_=tt, ps_=ps):
                    inst = None
                    for dc in range(16):
                        inst = h.matmul(ps_[:, :],
                                        afm[:, 1024 * dc + 128 * tt_:1024 * dc + 128 * (tt_ + 1)],
                                        gwos[:, 512 * dc:512 * (dc + 1)],
                                        start=(dc == 0), stop=(dc == 15))
                    return inst
                kb.op("pe", o2chain, reads=[ws], writes=[psl[tt % 2]])
                kb.op("dve", lambda h, ps_=ps, tt_=tt, g_=g4:
                      h.tensor_add(h1[:, 2048 * tt_ + 512 * g_:2048 * tt_ + 512 * (g_ + 1)],
                                   h1[:, 2048 * tt_ + 512 * g_:2048 * tt_ + 512 * (g_ + 1)],
                                   ps_[:, :]),
                      reads=[psl[tt % 2], hf_sl[tt]], writes=[hf_sl[tt]])
        TAP("hf1", half, h1, True, 16384)
        bar()

        hf_sl = [S(f"hf_{tt}") for tt in range(8)]
        xt2_sl = [S(f"xt2f_{dc}") for dc in range(16)]
        fmf_sl = [S(f"fmf_{dc}") for dc in range(16)]
        for tt in range(8):
            tile = h1[:, 2048 * tt:2048 * (tt + 1)]
            _, (rs_ap, rs_sl) = norm_stats(tile, hf_sl[tt], p7sq, S("p7sq"), False)
            kb.op("dve", lambda h, t_=tile, r=rs_ap:
                  h.tensor_scalar(xt2t[:, 0:2048], t_, r, None, ALU.mult),
                  reads=[rs_sl, hf_sl[tt]], writes=[S("xt2t")])
            for dc in range(16):
                pp = dc % 2
                kb.op("pe", lambda h, c=dc, p_=pp:
                      h.transpose(PS[p_][:, 0:128], xt2t[:, 128 * c:128 * (c + 1)], idf[:, :]),
                      reads=[S("xt2t")], writes=[psl[pp]])
                kb.op("dve", lambda h, c=dc, t_=tt, p_=pp:
                      h.tensor_scalar(xt2f[:, 1024 * c + 128 * t_:1024 * c + 128 * (t_ + 1)],
                                      PS[p_][:, 0:128], n2w[:, c:c + 1], None, ALU.mult),
                      reads=[psl[pp]], writes=[xt2_sl[dc]])
                kb.op("dve", lambda h, c=dc, p_=pp:
                      h.tensor_scalar(fmf[:, 128 * c:128 * (c + 1)], PS[p_][:, 0:128],
                                      n2w[:, c:c + 1], None, ALU.mult),
                      reads=[psl[pp]], writes=[fmf_sl[dc]])

            def gatechain(h):
                inst = None
                for dc in range(16):
                    inst = h.matmul(PS[2][:, 0:8], fmf[:, 128 * dc:128 * (dc + 1)],
                                    gsb[:, 8 * dc:8 * (dc + 1)], start=(dc == 0), stop=(dc == 15))
                return inst
            kb.op("pe", gatechain, reads=fmf_sl, writes=[psl[2]])
            kb.op("dve", lambda h: h.tensor_copy(gsc[:, 0:8], PS[2][:, 0:8]),
                  reads=[psl[2]], writes=[S("g_lg")])
            kb.op("dve", lambda h: h.reduce_max(scl[:, 9:10], gsc[:, 0:8], axis=AX.X),
                  reads=[S("g_lg")], writes=[S("g_m1")])
            kb.op("dve", lambda h: h.tensor_scalar(gsc[:, 8:16], gsc[:, 0:8], scl[:, 9:10],
                                                   None, ALU.is_equal),
                  reads=[S("g_lg"), S("g_m1")], writes=[S("g_e1")])
            kb.op("dve", lambda h: h.tensor_mul(gsc[:, 16:24], gsc[:, 8:16], i99[:, :]),
                  reads=[S("g_e1")], writes=[S("g_c1")])
            kb.op("dve", lambda h: h.tensor_reduce(scl[:, 10:11], gsc[:, 16:24], AX.X, ALU.min),
                  reads=[S("g_c1")], writes=[S("g_n1")])
            kb.op("dve", lambda h: h.tensor_scalar(gsc[:, 8:16], gsc[:, 16:24], scl[:, 10:11],
                                                   None, ALU.is_equal),
                  reads=[S("g_c1"), S("g_n1")], writes=[S("g_k1")])
            kb.op("dve", lambda h: h.scalar_tensor_tensor(gsc[:, 16:24], gsc[:, 8:16], -1e30,
                                                          gsc[:, 0:8], ALU.mult, ALU.add),
                  reads=[S("g_k1"), S("g_lg")], writes=[S("g_l2")])
            kb.op("dve", lambda h: h.reduce_max(scl[:, 11:12], gsc[:, 16:24], axis=AX.X),
                  reads=[S("g_l2")], writes=[S("g_m2")])
            kb.op("dve", lambda h: h.tensor_scalar(gsc[:, 24:32], gsc[:, 16:24], scl[:, 11:12],
                                                   None, ALU.is_equal),
                  reads=[S("g_l2"), S("g_m2")], writes=[S("g_e2")])
            kb.op("dve", lambda h: h.tensor_mul(gsc[:, 32:40], gsc[:, 24:32], i99[:, :]),
                  reads=[S("g_e2")], writes=[S("g_c2")])
            kb.op("dve", lambda h: h.tensor_reduce(scl[:, 12:13], gsc[:, 32:40], AX.X, ALU.min),
                  reads=[S("g_c2")], writes=[S("g_n2")])
            kb.op("dve", lambda h: h.tensor_scalar(gsc[:, 24:32], gsc[:, 32:40], scl[:, 12:13],
                                                   None, ALU.is_equal),
                  reads=[S("g_c2"), S("g_n2")], writes=[S("g_k2")])
            kb.op("dve", lambda h: h.tensor_scalar(scl[:, 13:14], scl[:, 11:12], scl[:, 9:10],
                                                   None, ALU.subtract),
                  reads=[S("g_m1"), S("g_m2")], writes=[S("g_d")])
            kb.op("act", lambda h: h.activation(scl[:, 14:15], scl[:, 13:14], AF.Exp),
                  reads=[S("g_d")], writes=[S("g_ed")])
            kb.op("dve", lambda h: h.tensor_scalar(scl[:, 13:14], scl[:, 14:15], 1.0, None,
                                                   ALU.add),
                  reads=[S("g_ed")], writes=[S("g_1e")])
            kb.op("dve", lambda h: h.reciprocal(scl[:, 15:16], scl[:, 13:14]),
                  reads=[S("g_1e")], writes=[S("g_p1")])
            kb.op("dve", lambda h: h.tensor_mul(scl[:, 13:14], scl[:, 14:15], scl[:, 15:16]),
                  reads=[S("g_ed"), S("g_p1")], writes=[S("g_p2")])
            kb.op("dve", lambda h: h.tensor_scalar(gsc[:, 40:48], gsc[:, 8:16], scl[:, 15:16],
                                                   None, ALU.mult),
                  reads=[S("g_k1"), S("g_p1")], writes=[S("g_wa")])
            kb.op("dve", lambda h, t_=tt:
                  h.scalar_tensor_tensor(wcol[:, 8 * t_:8 * (t_ + 1)], gsc[:, 24:32],
                                         scl[:, 13:14], gsc[:, 40:48], ALU.mult, ALU.add),
                  reads=[S("g_k2"), S("g_p2"), S("g_wa")], writes=[S(f"wcol_{tt}")])
            dma("sp", cbuf[:, 2048 * tt:2048 * (tt + 1)], sH1[128 * tt:128 * (tt + 1), :],
                writes=[S(f"c_{tt}")])
            kb.op("dve", lambda h, t_=tt:
                  h.tensor_add(cbuf[:, 2048 * t_:2048 * (t_ + 1)],
                               cbuf[:, 2048 * t_:2048 * (t_ + 1)],
                               h1[:, 2048 * t_:2048 * (t_ + 1)]),
                  reads=[S(f"c_{tt}"), hf_sl[tt]], writes=[S(f"c_{tt}")])
        TAP("xt2f", half, xt2f, False, 16384)
        TAP("wcol", half, wcol, True, 64)
        TAP("cbase", half, cbuf, True, 16384)
        bar()

        xt2_sl = [S(f"xt2f_{dc}") for dc in range(16)]
        c_sl = [S(f"c_{tt}") for tt in range(8)]
        wc_sl = [S(f"wcol_{tt}") for tt in range(8)]
        for e in range(NE):
            for fq in range(4):
                moeh_sl = [S(f"mh_{fc}") for fc in range(8)]
                for fc in range(8):
                    par = fc % 2
                    w1s = S(f"wu1_{par}")
                    w3s = S(f"wu3_{par}")
                    dma("pool", bass.AP(wu1s, 2048 * par, [[4096, 128], [128, 16], [1, 128]]),
                        fm_src3(hW1, e, FF, 1024 * fq + 128 * fc, 128), writes=[w1s])
                    dma("pool", bass.AP(wu3s, 2048 * par, [[4096, 128], [128, 16], [1, 128]]),
                        fm_src3(hW3, e, FF, 1024 * fq + 128 * fc, 128), writes=[w3s])
                    for t5 in range(2):
                        def u1c(h, p_=par, t_=t5):
                            inst = None
                            for dc in range(16):
                                inst = h.matmul(
                                    PS[t_][:, :],
                                    wu1s[:, 2048 * p_ + 128 * dc:2048 * p_ + 128 * (dc + 1)],
                                    xt2f[:, 1024 * dc + 512 * t_:1024 * dc + 512 * (t_ + 1)],
                                    start=(dc == 0), stop=(dc == 15))
                            return inst
                        kb.op("pe", u1c, reads=[w1s] + xt2_sl, writes=[psl[t5]])

                        def u3c(h, p_=par, t_=t5):
                            inst = None
                            for dc in range(16):
                                inst = h.matmul(
                                    PS[2 + t_][:, :],
                                    wu3s[:, 2048 * p_ + 128 * dc:2048 * p_ + 128 * (dc + 1)],
                                    xt2f[:, 1024 * dc + 512 * t_:1024 * dc + 512 * (t_ + 1)],
                                    start=(dc == 0), stop=(dc == 15))
                            return inst
                        kb.op("pe", u3c, reads=[w3s] + xt2_sl, writes=[psl[2 + t5]])
                        kb.op("act", lambda h, t_=t5:
                              h.activation(slb[:, 512 * t_:512 * (t_ + 1)], PS[t_][:, :], AF.Silu),
                              reads=[psl[t5]], writes=[S("slb")])
                        kb.op("dve", lambda h, fc_=fc, t_=t5:
                              h.tensor_mul(moeh[:, 1024 * fc_ + 512 * t_:1024 * fc_ + 512 * (t_ + 1)],
                                           slb[:, 512 * t_:512 * (t_ + 1)], PS[2 + t_][:, :]),
                              reads=[S("slb"), psl[2 + t5]], writes=[moeh_sl[fc]])
                w2_sl = [S(f"w2r_{fc}") for fc in range(8)]
                for fc in range(8):
                    dma("pool", w2r[:, 2048 * fc:2048 * (fc + 1)],
                        bass.AP(hW2, (e * FF + 1024 * fq + 128 * fc) * D,
                                [[D, 128], [1, D]]),
                        writes=[w2_sl[fc]])
                for tt in range(8):
                    eeng = "dve" if tt % 2 == 0 else "pool"
                    for g4 in range(4):
                        pp = 4 + g4 % 2

                        def w2c(h, tt_=tt, g_=g4, p_=pp):
                            inst = None
                            for fc in range(8):
                                inst = h.matmul(
                                    PS[p_][:, :],
                                    moeh[:, 1024 * fc + 128 * tt_:1024 * fc + 128 * (tt_ + 1)],
                                    w2r[:, 2048 * fc + 512 * g_:2048 * fc + 512 * (g_ + 1)],
                                    start=(fc == 0), stop=(fc == 7))
                            return inst
                        kb.op("pe", w2c, reads=moeh_sl + w2_sl, writes=[psl[pp]])
                        kb.op("dve", lambda h, tt_=tt, g_=g4, e_=e, p_=pp:
                              h.scalar_tensor_tensor(
                                  cbuf[:, 2048 * tt_ + 512 * g_:2048 * tt_ + 512 * (g_ + 1)],
                                  PS[p_][:, :], wcol[:, 8 * tt_ + e_:8 * tt_ + e_ + 1],
                                  cbuf[:, 2048 * tt_ + 512 * g_:2048 * tt_ + 512 * (g_ + 1)],
                                  ALU.mult, ALU.add),
                              reads=[psl[pp], wc_sl[tt], c_sl[tt]], writes=[c_sl[tt]])
        TAP("cmoe", half, cbuf, True, 16384)
        bar()

        c_sl = [S(f"c_{tt}") for tt in range(8)]
        for tt in range(8):
            tile = cbuf[:, 2048 * tt:2048 * (tt + 1)]
            (mu_ap, mu_sl), (rs_ap, rs_sl) = norm_stats(tile, c_sl[tt], p7sq, S("p7sq"), True)
            kb.op("dve", lambda h, t_=tile, m=mu_ap, r=rs_ap:
                  h.tensor_scalar(t_, t_, m, r, ALU.subtract, ALU.mult),
                  reads=[mu_sl, rs_sl, c_sl[tt]], writes=[c_sl[tt]])
            store_recs.append(
                dma("sp", hOut[hb + 128 * tt:hb + 128 * (tt + 1), :], tile, reads=[c_sl[tt]]))
        bar()

    kb.op("sp", lambda h: h.nop(), deps=store_recs)

    kb.finalize(nc, ctx)
    with ctx:
        with nc.Block() as block:
            @block.sync
            def _(h):
                kb.emit("sp", h)

            @block.scalar
            def _(h):
                kb.emit("act", h)

            @block.vector
            def _(h):
                kb.emit("dve", h)

            @block.gpsimd
            def _(h):
                kb.emit("pool", h)

            @block.tensor
            def _(h):
                kb.emit("pe", h)
    return nc


_CACHE = {}


def kernel(**inputs):
    if "nc" not in _CACHE:
        _CACHE["nc"] = build()
    nc = _CACHE["nc"]
    hc = host_consts()
    x = np.ascontiguousarray(np.asarray(inputs["L0_attn_x"]), dtype=np.float32)

    def pc(v):
        return np.ascontiguousarray(np.asarray(v).reshape(16, 128).T, dtype=np.float32)

    shared = dict(
        gWq=np.ascontiguousarray(np.asarray(inputs["L0_attn_Wq"]), np.float32),
        gWk=np.ascontiguousarray(np.asarray(inputs["L0_attn_Wk"]), np.float32),
        gWv=np.ascontiguousarray(np.asarray(inputs["L0_attn_Wv"]), np.float32),
        gWg=np.ascontiguousarray(np.asarray(inputs["L0_attn_Wg"]), np.float32),
        gWo=np.ascontiguousarray(np.asarray(inputs["L0_attn_Wo"]), np.float32),
        gamma=pc(inputs["L0_attn_gamma"]),
        beta=pc(inputs["L0_attn_beta"]),
        n1w=pc(inputs["L0_ffn_norm1_w"]),
        n2w=pc(inputs["L0_ffn_norm2_w"]),
        fwq=np.ascontiguousarray(np.asarray(inputs["L0_ffn_wq"]), np.float32),
        fwk=np.ascontiguousarray(np.asarray(inputs["L0_ffn_wk"]), np.float32),
        fwv=np.ascontiguousarray(np.asarray(inputs["L0_ffn_wv"]), np.float32),
        fwo=np.ascontiguousarray(np.asarray(inputs["L0_ffn_wo"]), np.float32),
        gatew=np.ascontiguousarray(np.asarray(inputs["L0_ffn_gate_w"]), np.float32),
        w1=np.ascontiguousarray(np.asarray(inputs["L0_ffn_w1"]), np.float32),
        w3=np.ascontiguousarray(np.asarray(inputs["L0_ffn_w3"]), np.float32),
        w2=np.ascontiguousarray(np.asarray(inputs["L0_ffn_w2"]), np.float32),
        id_bf=hc["id_bf"], id_f32=hc["id_f32"], swapP=hc["swapP"],
        maskA=hc["maskA"], emaster=hc["emaster"],
        ropec=hc["ropec"], ropes=hc["ropes"], i99=hc["i99"],
    )
    in_maps = []
    for c in range(8):
        m = dict(shared)
        m["x"] = np.ascontiguousarray(x[c % 4])
        in_maps.append(m)
    res = run_bass_kernel_spmd(nc, in_maps, core_ids=list(range(8)))
    return np.stack([np.asarray(res.results[c]["out"], dtype=np.float32) for c in range(4)])


if __name__ == "__main__":
    import time
    t0 = time.time()
    nc = build()
    print(f"built in {time.time() - t0:.1f}s; recs:",
          {e: 0 for e in ENGS})
